# version 07
# baseline (speedup 1.0000x reference)
import contextlib
import numpy as np
import ml_dtypes

import concourse.bass as bass
import concourse.mybir as mybir
from concourse.bass_utils import run_bass_kernel_spmd

F32 = mybir.dt.float32
BF16 = mybir.dt.bfloat16
AF = mybir.ActivationFunctionType
ALU = mybir.AluOpType
AX = mybir.AxisListType.X

L = 4096
D = 1024
DC = 8
TC = 32
NQ = 32
DFF = 4096
FC = 32
NE = 8
BAND = 384
GW = 2 * BAND + 128
SUP = 1024
NS = L // SUP
NT = SUP // 128
GAMMA = 0.9
EPS = 1e-5

ENG_SEMS = {"tensor": "T", "vector": "V", "scalar": "S"}


class _Sentinel:
    def __repr__(self):
        return "PENDING"


PENDING = _Sentinel()


class Sched:
    ENGINES = ("tensor", "vector", "scalar", "gpsimd", "sync")

    def __init__(self):
        self.steps = {e: [] for e in self.ENGINES}
        self.cnt = {}
        self.own = {}
        self.readers = {}
        self.seen = {e: {} for e in self.ENGINES}
        self.pending_t = []

    def mark(self, sem):
        return self.cnt.get(sem, 0)

    def emit(self, eng, fn, reads=(), writes=(), waits=(), inc=False, dma_sem=None):
        need = {}

        def add(pos):
            s, v = pos[0], pos[1]
            if not v:
                return
            if eng == "tensor" and s == "T":
                return
            assert v is not PENDING, "edge to unresolved tensor effect"
            if v > need.get(s, 0):
                need[s] = v

        for s, v in waits:
            add((s, v))
        for k in reads:
            o = self.own.get(k)
            if o:
                add(o)
        for k in writes:
            o = self.own.get(k)
            if o:
                add(o)
            for p in self.readers.get(k, {}).values():
                add(p)

        ws = []
        for s, v in sorted(need.items()):
            have = self.cnt.get(s, 0)
            assert v <= have, f"wait {s}>={v} but only {have} emitted"
            if s not in ("T", "V", "S"):
                assert v == have, f"prefix wait on DMA sem {s}: {v} < {have}"
            if self.seen[eng].get(s, 0) >= v:
                continue
            self.seen[eng][s] = v
            ws.append((s, v))

        incs = ()
        if dma_sem is not None:
            self.cnt[dma_sem] = self.cnt.get(dma_sem, 0) + 16
            incs = ((dma_sem, 16),)
            pos_ref = [dma_sem, self.cnt[dma_sem]]
        elif eng == "tensor":
            if inc:
                self.cnt["T"] = self.cnt.get("T", 0) + 1
                incs = (("T", 1),)
                mk = self.cnt["T"]
                for rec in self.pending_t:
                    rec[1] = mk
                self.pending_t = []
                pos_ref = ["T", mk]
            else:
                pos_ref = ["T", PENDING]
                self.pending_t.append(pos_ref)
        elif eng in ENG_SEMS:
            sem = ENG_SEMS[eng]
            self.cnt[sem] = self.cnt.get(sem, 0) + 1
            incs = ((sem, 1),)
            pos_ref = [sem, self.cnt[sem]]
        else:
            pos_ref = ["", 0]

        self.steps[eng].append((tuple(ws), fn, incs))

        for k in writes:
            self.own[k] = pos_ref
            self.readers[k] = {}
        for k in reads:
            self.readers.setdefault(k, {})[id(pos_ref)] = pos_ref
        return self.cnt.get(dma_sem if dma_sem is not None else
                            ENG_SEMS.get(eng, ""), 0)


def _constants():
    i = np.arange(512, dtype=np.float64)
    scale_vec = (2 * i + 0.4 * D) / (1.4 * D)
    inv_freq = 1.0 / 10000.0 ** (i / 512.0)
    n = np.arange(L, dtype=np.float64)
    scale = scale_vec[None, :] ** (n[:, None] / 512.0)
    sinu = n[:, None] * inv_freq[None, :]
    cq = np.ascontiguousarray((np.cos(sinu) * scale).T.astype(np.float32))
    sq = np.ascontiguousarray((np.sin(sinu) * scale).T.astype(np.float32))
    ck = np.ascontiguousarray((np.cos(sinu) / scale).T.astype(np.float32))
    sk = np.ascontiguousarray((np.sin(sinu) / scale).T.astype(np.float32))
    ii = np.arange(128, dtype=np.float64)[:, None]
    jj = np.arange(GW, dtype=np.float64)[None, :]
    g = (GAMMA ** np.abs(BAND + ii - jj)).astype(np.float32)
    ident = np.eye(128, dtype=ml_dtypes.bfloat16)
    onesf = np.ones((1, 128), dtype=np.float32)
    perm = np.concatenate([np.arange(0, D, 2), np.arange(1, D, 2)])
    return cq, sq, ck, sk, g, ident, onesf, perm


def win_chunks(t):
    return max(0, t - BAND // 128), min(TC, t + 1 + BAND // 128)


def build(taps=()):
    nc = bass.Bass("TRN2", target_bir_lowering=False)

    xt = nc.dram_tensor("xt", [D, L], F32, kind="ExternalInput").ap()
    xtok = nc.dram_tensor("xtok", [L, D], F32, kind="ExternalInput").ap()
    wq_d = nc.dram_tensor("wq", [D, D], F32, kind="ExternalInput").ap()
    wk_d = nc.dram_tensor("wk", [D, D], F32, kind="ExternalInput").ap()
    wv_d = nc.dram_tensor("wv", [D, D], F32, kind="ExternalInput").ap()
    cq_d = nc.dram_tensor("cq", [512, L], F32, kind="ExternalInput").ap()
    sq_d = nc.dram_tensor("sq", [512, L], F32, kind="ExternalInput").ap()
    ck_d = nc.dram_tensor("ck", [512, L], F32, kind="ExternalInput").ap()
    sk_d = nc.dram_tensor("sk", [512, L], F32, kind="ExternalInput").ap()
    g_d = nc.dram_tensor("gband", [128, GW], F32, kind="ExternalInput").ap()
    id_d = nc.dram_tensor("ident", [128, 128], BF16, kind="ExternalInput").ap()
    ones_d = nc.dram_tensor("onesf", [1, 128], F32, kind="ExternalInput").ap()
    rw_d = nc.dram_tensor("rw", [D, NE], F32, kind="ExternalInput").ap()
    rbb_d = nc.dram_tensor("rbb", [128, NE], F32, kind="ExternalInput").ap()
    w1_d = nc.dram_tensor("w1", [NE, D, DFF], F32, kind="ExternalInput").ap()
    b1_d = nc.dram_tensor("b1", [128, NE * FC], F32, kind="ExternalInput").ap()
    w2_d = nc.dram_tensor("w2", [NE, DFF, D], F32, kind="ExternalInput").ap()
    b2_d = nc.dram_tensor("b2", [NE, D], F32, kind="ExternalInput").ap()
    out_d = nc.dram_tensor("out", [L, D], F32, kind="ExternalOutput").ap()
    x2t_d = nc.dram_tensor("x2t_s", [D, L], BF16, kind="Internal").ap()
    x2k_d = nc.dram_tensor("x2k_s", [L, D], BF16, kind="Internal").ap()

    tap_aps = {}
    for name, shape, dt in taps:
        tap_aps[name] = nc.dram_tensor(name, shape, dt, kind="ExternalOutput").ap()

    SB_BASE = 18432
    SB_LIMIT = 229376 - 256

    def nbytes(shape, dtype):
        return int(np.prod(shape[1:])) * (2 if dtype == BF16 else 4)

    class Bump:
        def __init__(self, base, limit=SB_LIMIT):
            self.off = base
            self.limit = limit

        def sb(self, name, shape, dtype):
            self.off = (self.off + 31) & ~31
            t = nc.alloc_sbuf_tensor_at(name, shape, dtype, offset=self.off)
            self.off += nbytes(shape, dtype)
            assert self.off <= self.limit, (name, self.off)
            return t

    P = Bump(SB_BASE)
    g_s = P.sb("g_s", [128, GW], F32)
    id_s = P.sb("id_s", [128, 128], BF16)
    ones_s = P.sb("ones_s", [1, 128], F32)
    rbb_s = P.sb("rbb_s", [128, NE], F32)
    rwb_s = P.sb("rwb_s", [128, DC * NE], BF16)
    b1_s = P.sb("b1_s", [128, NE * FC], F32)
    qf_s = P.sb("qf_s", [128, D], F32)
    rsc_s = P.sb("rsc_s", [128, 512], F32)
    mv_s = P.sb("mv_s", [128, 2], F32)
    bst_s = P.sb("bst_s", [128, 12], F32)
    s1_s = P.sb("s1_s", [128, 16], F32)
    zq_s = P.sb("zq_s", [128, 4], F32)
    mq_s = P.sb("mq_s", [128, 4], F32)
    dq_s = P.sb("dq_s", [128, 4], F32)
    eq_s = P.sb("eq_s", [128, 4], F32)
    rq_s = P.sb("rq_s", [128, 4], F32)
    PERSIST_END = P.off

    A = Bump(PERSIST_END)
    kt_s = [A.sb(f"kt{c}", [128, L], BF16) for c in range(DC)]
    v_s = [A.sb(f"v{c}", [128, D], BF16) for c in range(TC)]
    PH0A_END = A.off

    Z = Bump(PH0A_END)
    w0_s = Z.sb("w0_s", [128, 2 * DC * D], BF16)
    xsl_s = [Z.sb(f"xsl{i}", [128, DC * 512], BF16) for i in range(2)]
    kf_s = Z.sb("kf_s", [128, 2048], F32)
    cs_s = [Z.sb(f"cs{i}", [128, 1024], F32) for i in range(2)]

    Z = Bump(PH0A_END)
    wqb_s = Z.sb("wqb_s", [128, DC * D], BF16)
    xq_s = [Z.sb(f"xq{i}", [128, D], BF16) for i in range(2)]
    cqs_s = [Z.sb(f"cqs{i}", [128, 512], F32) for i in range(2)]
    sqs_s = [Z.sb(f"sqs{i}", [128, 512], F32) for i in range(2)]
    xr_s = [Z.sb(f"xr{i}", [128, D], F32) for i in range(2)]
    exp_s = Z.sb("exp_s", [128, L], BF16)
    pb_s = Z.sb("pb_s", [128, GW], BF16)
    qt_s = Z.sb("qt_s", [128, D], BF16)
    pt_s = [Z.sb(f"pt{i}", [128, 128], BF16) for i in range(7)]
    v1_s = Z.sb("v1_s", [128, D], F32)
    x2k_t = Z.sb("x2k_t", [128, D], BF16)
    x2t_t = Z.sb("x2t_t", [128, D], BF16)

    Z = Bump(PERSIST_END)
    x2ts_s = [Z.sb(f"x2ts{i}", [128, DC * SUP], BF16) for i in range(2)]
    x2ks_s = Z.sb("x2ks_s", [128, NT * D], BF16)
    h_s = [Z.sb(f"h{i}", [128, SUP], BF16) for i in range(8)]
    acc_s = [Z.sb(f"acc{i}", [128, D], F32) for i in range(NT)]
    b2b_s = Z.sb("b2b_s", [128, D], F32)
    b2r_s = [Z.sb(f"b2r{i}", [1, D], F32) for i in range(2)]
    v2_s = Z.sb("v2_s", [128, D], F32)
    lg_s = Z.sb("lg_s", [128, NE], F32)
    oh1_s = Z.sb("oh1_s", [128, NE], F32)
    oh2_s = Z.sb("oh2_s", [128, NE], F32)
    msk_s = Z.sb("msk_s", [128, NE], F32)
    c_s = [Z.sb(f"c_s{i}", [128, NT * NE], F32) for i in range(2)]
    outst_s = [Z.sb(f"outst{i}", [128, D], F32) for i in range(2)]
    w1g_s = [Z.sb(f"w1g{i}", [128, 8 * 1024], BF16) for i in range(2)]
    w2g_s = [Z.sb(f"w2g{i}", [128, 8 * 1024], BF16) for i in range(2)]

    psA = nc.alloc_psum_tensor("psA", [128, 1024], F32)
    psA2 = nc.alloc_psum_tensor("psA2", [128, 1024], F32)
    psB = nc.alloc_psum_tensor("psB", [128, 1024], F32)
    psT = nc.alloc_psum_tensor("psT", [128, 1024], BF16)
    psC = nc.alloc_psum_tensor("psC", [128, 512], F32)

    S = Sched()
    M = {}

    def dma(eng, sem, out, in_, reads=(), writes=(), waits=()):
        return S.emit(eng, lambda e, o=out, i=in_: e.dma_start(out=o, in_=i),
                      reads=reads, writes=writes, waits=waits, dma_sem=sem)

    def mm(out, lhsT, rhs, start, stop, reads=(), writes=(), waits=(), inc=False):
        return S.emit("tensor",
                      lambda e, o=out, l=lhsT, r=rhs, st=start, sp=stop:
                      e.matmul(o, l, r, start=st, stop=sp),
                      reads=reads, writes=writes, waits=waits, inc=inc)

    dma("sync", "pre_g", g_s[:], g_d, writes=["g"])
    dma("sync", "pre_id", id_s[:], id_d, writes=["id"])
    dma("sync", "pre_on", ones_s[:], ones_d, writes=["ones"])
    dma("sync", "pre_rb", rbb_s[:], rbb_d, writes=["rbb"])
    dma("sync", "pre_b1", b1_s[:], b1_d, writes=["b1"])
    dma("gpsimd", "rwb", rwb_s[:].rearrange("p (c e) -> p c e", e=NE),
        rw_d.rearrange("(c p) e -> p c e", p=128), writes=["rwb"])
    dma("gpsimd", "gw", w0_s[:, 0:DC * D].rearrange("p (c n) -> p c n", n=D),
        wk_d.rearrange("(c p) n -> p c n", p=128), writes=["w0"])
    dma("gpsimd", "gwv", w0_s[:, DC * D:2 * DC * D].rearrange("p (c n) -> p c n", n=D),
        wv_d.rearrange("(c p) n -> p c n", p=128), writes=["w0v"])

    def xslab_dma(slot, tb):
        src = xt.rearrange("(c p) n -> p c n", p=128)[:, :, tb * 512:(tb + 1) * 512]
        dst = xsl_s[slot][:].rearrange("p (c n) -> p c n", n=512)
        return dma("gpsimd", f"xsl{slot}", dst, src, writes=[f"xsl{slot}"])

    for tb in range(8):
        xslab_dma(tb % 2, tb)
        for pr in range(4):
            idx = tb * 4 + pr
            csl = idx % 2
            dma("sync", f"cs{csl}", cs_s[csl][:, 0:512],
                ck_d[pr * 128:(pr + 1) * 128, tb * 512:(tb + 1) * 512],
                writes=[f"cs{csl}"])
            dma("sync", f"cs{csl}", cs_s[csl][:, 512:1024],
                sk_d[pr * 128:(pr + 1) * 128, tb * 512:(tb + 1) * 512],
                writes=[f"cs{csl}"])
            for half, do in ((0, pr), (1, pr + 4)):
                for di in range(DC):
                    mm(psA[:, half * 512:(half + 1) * 512],
                       w0_s[:, di * D + do * 128: di * D + do * 128 + 128],
                       xsl_s[tb % 2][:, di * 512:(di + 1) * 512],
                       start=(di == 0), stop=(di == DC - 1),
                       reads=["w0", f"xsl{tb % 2}"], writes=[f"psA.{half}"],
                       inc=(di == DC - 1))
            for half in range(2):
                S.emit("scalar",
                       lambda e, o=kf_s[:, half * 512:(half + 1) * 512],
                              i=psA[:, half * 512:(half + 1) * 512]: e.copy(o, i),
                       reads=[f"psA.{half}"], writes=[f"kf.e{half}"])
            tc = idx
            tc4 = pr
            pk = "psA2" if tc % 2 == 0 else "psB"
            ps = psA2 if tc % 2 == 0 else psB
            for di in range(DC):
                lhsT = xsl_s[tb % 2][:, di * 512 + tc4 * 128: di * 512 + tc4 * 128 + 128]
                for nb in range(2):
                    mm(ps[:, nb * 512:(nb + 1) * 512], lhsT,
                       w0_s[:, DC * D + di * D + nb * 512:
                            DC * D + di * D + (nb + 1) * 512],
                       start=(di == 0), stop=(di == DC - 1),
                       reads=["w0v", f"xsl{tb % 2}"],
                       writes=[f"{pk}.0", f"{pk}.1"] if nb == 0 and di == 0 else
                              [f"{pk}.{nb}"],
                       inc=(di == DC - 1 and nb == 1))
            S.emit("vector", lambda e, o=v_s[tc][:], i=ps[:]: e.tensor_copy(o, i),
                   reads=[f"{pk}.0", f"{pk}.1"], writes=[f"v{tc}"])
            ke, ko = kf_s[:, 0:512], kf_s[:, 512:1024]
            t1, t2 = kf_s[:, 1024:1536], kf_s[:, 1536:2048]
            Ct, St = cs_s[csl][:, 0:512], cs_s[csl][:, 512:1024]
            colr = slice(tb * 512, (tb + 1) * 512)
            S.emit("vector", lambda e, o=t1, a=ke, b=Ct: e.tensor_mul(o, a, b),
                   reads=["kf.e0", f"cs{csl}"], writes=["kf.t1"])
            S.emit("vector", lambda e, o=t2, a=ko, b=St: e.tensor_mul(o, a, b),
                   reads=["kf.e1", f"cs{csl}"], writes=["kf.t2"])
            S.emit("vector", lambda e, o=kt_s[pr][:, colr], a=t1, b=t2:
                   e.tensor_sub(o, a, b),
                   reads=["kf.t1", "kf.t2"], writes=[f"kt{pr}"])
            S.emit("vector", lambda e, o=t1, a=ko, b=Ct: e.tensor_mul(o, a, b),
                   reads=["kf.e1", f"cs{csl}"], writes=["kf.t1"])
            S.emit("vector", lambda e, o=t2, a=ke, b=St: e.tensor_mul(o, a, b),
                   reads=["kf.e0", f"cs{csl}"], writes=["kf.t2"])
            S.emit("vector", lambda e, o=kt_s[pr + 4][:, colr], a=t1, b=t2:
                   e.tensor_add(o, a, b),
                   reads=["kf.t1", "kf.t2"], writes=[f"kt{pr + 4}"])
    M["ph0_T"] = S.mark("T")
    M["ph0_V"] = S.mark("V")
    M["ph0_S"] = S.mark("S")
    M["k_done_V"] = S.mark("V")
    PH_BAR0 = [("T", M["ph0_T"]), ("V", M["ph0_V"]), ("S", M["ph0_S"])]

    if "dbg_kt" in tap_aps:
        for c in range(DC):
            dma("sync", "tapw", tap_aps["dbg_kt"][c * 128:(c + 1) * 128, :], kt_s[c][:],
                reads=[f"kt{c}"])
    if "dbg_v" in tap_aps:
        for tc in range(TC):
            dma("sync", "tapw", tap_aps["dbg_v"][tc * 128:(tc + 1) * 128, :], v_s[tc][:],
                reads=[f"v{tc}"])

    dma("gpsimd", "gw", wqb_s[:].rearrange("p (c n) -> p c n", n=D),
        wq_d.rearrange("(c p) n -> p c n", p=128), writes=["wqb"],
        waits=PH_BAR0)

    def tile_dmas(t):
        slot = t % 2
        bar = PH_BAR0 if t < 2 else ()
        src = xt.rearrange("(c p) n -> p c n", p=128)[:, :, t * 128:(t + 1) * 128]
        dma("gpsimd", f"xq{slot}", xq_s[slot][:].rearrange("p (c n) -> p c n", n=128),
            src, writes=[f"xq{slot}"], waits=bar)
        dma("sync", f"cqs{slot}", cqs_s[slot][:].rearrange("p (c n) -> p c n", n=128),
            cq_d.rearrange("(c p) n -> p c n", p=128)[:, :, t * 128:(t + 1) * 128],
            writes=[f"cqs{slot}"], waits=bar)
        dma("sync", f"cqs{slot}", sqs_s[slot][:].rearrange("p (c n) -> p c n", n=128),
            sq_d.rearrange("(c p) n -> p c n", p=128)[:, :, t * 128:(t + 1) * 128],
            writes=[f"cqs{slot}"])
        dma("sync", f"xr{slot}", xr_s[slot][:], xtok[t * 128:(t + 1) * 128, :],
            writes=[f"xr{slot}"], waits=bar)

    def qproj(t):
        slot = t % 2
        for do in range(DC):
            for di in range(DC):
                mm(psB[:, do * 128:(do + 1) * 128],
                   wqb_s[:, di * D + do * 128: di * D + do * 128 + 128],
                   xq_s[slot][:, di * 128:(di + 1) * 128],
                   start=(di == 0), stop=(di == DC - 1),
                   reads=["wqb", f"xq{slot}"],
                   writes=["psB.0", "psB.1"] if (do == 0 and di == 0) else
                          [f"psB.{do // 4}"],
                   inc=(di == DC - 1))

    def rot_q(t):
        slot = t % 2
        for pr in range(4):
            qe = psB[:, pr * 128:(pr + 1) * 128]
            qo = psB[:, (pr + 4) * 128:(pr + 5) * 128]
            Ct = cqs_s[slot][:, pr * 128:(pr + 1) * 128]
            St = sqs_s[slot][:, pr * 128:(pr + 1) * 128]
            t1, t2 = rsc_s[:, 0:128], rsc_s[:, 128:256]
            t3, t4 = rsc_s[:, 256:384], rsc_s[:, 384:512]
            S.emit("vector", lambda e, o=t1, a=qe, b=Ct: e.tensor_mul(o, a, b),
                   reads=["psB.0", f"cqs{slot}"], writes=["rsc.1"],
                   waits=PH_BAR0 if t == 0 else ())
            S.emit("vector", lambda e, o=t2, a=qo, b=St: e.tensor_mul(o, a, b),
                   reads=["psB.1", f"cqs{slot}"], writes=["rsc.2"])
            S.emit("vector", lambda e, o=qt_s[:, pr * 128:(pr + 1) * 128], a=t1, b=t2:
                   e.tensor_sub(o, a, b), reads=["rsc.1", "rsc.2"], writes=[f"qt.{pr}"])
            S.emit("vector", lambda e, o=t3, a=qo, b=Ct: e.tensor_mul(o, a, b),
                   reads=["psB.1", f"cqs{slot}"], writes=["rsc.3"])
            S.emit("vector", lambda e, o=t4, a=qe, b=St: e.tensor_mul(o, a, b),
                   reads=["psB.0", f"cqs{slot}"], writes=["rsc.4"])
            S.emit("vector", lambda e, o=qt_s[:, (pr + 4) * 128:(pr + 5) * 128],
                          a=t3, b=t4:
                   e.tensor_add(o, a, b), reads=["rsc.3", "rsc.4"],
                   writes=[f"qt.{pr + 4}"])

    def squarter(t, q):
        pk = "psA" if q % 2 == 0 else "psA2"
        ps = psA if q % 2 == 0 else psA2
        for kb in range(2):
            for di in range(DC):
                mm(ps[:, kb * 512:(kb + 1) * 512],
                   qt_s[:, di * 128:(di + 1) * 128],
                   kt_s[di][:, q * 1024 + kb * 512: q * 1024 + (kb + 1) * 512],
                   start=(di == 0), stop=(di == DC - 1),
                   reads=[f"qt.{di}", f"kt{di}"],
                   writes=[f"{pk}.0", f"{pk}.1"] if (kb == 0 and di == 0) else
                          [f"{pk}.{kb}"],
                   inc=(di == DC - 1 and kb == 1))

    def max_q(t, q):
        pk = "psA" if q % 2 == 0 else "psA2"
        ps = psA if q % 2 == 0 else psA2
        S.emit("vector", lambda e, o=mq_s[:, q:q + 1], i=ps[:]:
               e.reduce_max(o, i, axis=AX, negate=True),
               reads=[f"{pk}.0", f"{pk}.1"], writes=[f"mq.{q}"])

    def exp_q(t, q):
        pk = "psA" if q % 2 == 0 else "psA2"
        ps = psA if q % 2 == 0 else psA2
        S.emit("scalar",
               lambda e, o=exp_s[:, q * 1024:(q + 1) * 1024], i=ps[:],
                      b=mq_s[:, q:q + 1], z=zq_s[:, q:q + 1]:
               e.activation(o, i, AF.Exp, bias=b, accum_out=z),
               reads=[f"{pk}.0", f"{pk}.1", f"mq.{q}"],
               writes=[f"exp.{q}", f"zq.{q}"])

    def softmax_tail(t):
        a, b, mn = s1_s[:, 0:1], s1_s[:, 1:2], s1_s[:, 2:3]
        S.emit("vector", lambda e, o=a, x=mq_s[:, 0:1], y=mq_s[:, 1:2]:
               e.tensor_tensor(o, x, y, ALU.min),
               reads=["mq.0", "mq.1"], writes=["s1.0"])
        S.emit("vector", lambda e, o=b, x=mq_s[:, 2:3], y=mq_s[:, 3:4]:
               e.tensor_tensor(o, x, y, ALU.min),
               reads=["mq.2", "mq.3"], writes=["s1.1"])
        S.emit("vector", lambda e, o=mn, x=a, y=b: e.tensor_tensor(o, x, y, ALU.min),
               reads=["s1.0", "s1.1"], writes=["s1.2"])
        for q in range(4):
            S.emit("vector", lambda e, o=dq_s[:, q:q + 1], x=mn, y=mq_s[:, q:q + 1]:
                   e.tensor_sub(o, x, y), reads=["s1.2", f"mq.{q}"], writes=[f"dq.{q}"])
        S.emit("scalar", lambda e, o=eq_s[:], i=dq_s[:]: e.activation(o, i, AF.Exp),
               reads=["dq.0", "dq.1", "dq.2", "dq.3"], writes=["eq"])
        t4, zz, rz = s1_s[:, 4:8], s1_s[:, 8:9], s1_s[:, 9:10]
        S.emit("vector", lambda e, o=t4, x=zq_s[:], y=eq_s[:]: e.tensor_mul(o, x, y),
               reads=["zq.0", "zq.1", "zq.2", "zq.3", "eq"], writes=["s1.47"])
        S.emit("vector", lambda e, o=zz, i=t4: e.reduce_sum(o, i, axis=AX),
               reads=["s1.47"], writes=["s1.8"])
        S.emit("vector", lambda e, o=rz, i=zz: e.reciprocal(o, i),
               reads=["s1.8"], writes=["s1.9"])
        S.emit("vector", lambda e, o=rq_s[:], i=eq_s[:], s=rz:
               e.tensor_scalar(o, i, s, None, op0=ALU.mult),
               reads=["eq", "s1.9"], writes=["rq"])
        lo, hi = win_chunks(t)
        k0 = lo * 128
        goff = BAND - (t * 128 - k0)
        segs = []
        for q in range(4):
            s0 = max(k0, q * 1024)
            s1e = min(hi * 128, (q + 1) * 1024)
            if s0 < s1e:
                segs.append((q, s0, s1e))
        for q, s0, s1e in segs:
            S.emit("vector",
                   lambda e, o=pb_s[:, s0 - k0: s1e - k0],
                          x=exp_s[:, s0:s1e], s=rq_s[:, q:q + 1],
                          gg=g_s[:, goff + (s0 - k0): goff + (s1e - k0)]:
                   e.scalar_tensor_tensor(o, x, s, gg, op0=ALU.mult, op1=ALU.mult),
                   reads=[f"exp.{q}", "rq", "g"], writes=["pb"])

    def transp(t):
        lo, hi = win_chunks(t)
        nw = hi - lo
        for i in range(nw):
            S.emit("tensor",
                   lambda e, o=psT[:, i * 128:(i + 1) * 128],
                          x=pb_s[:, i * 128:(i + 1) * 128], idn=id_s[:]:
                   e.transpose(o, x, idn),
                   reads=["pb", "id"], writes=["psT"], inc=(i == nw - 1))
        for i in range(nw):
            S.emit("scalar",
                   lambda e, o=pt_s[i][:], x=psT[:, i * 128:(i + 1) * 128]: e.copy(o, x),
                   reads=["psT"], writes=[f"pt{i}"])

    def pv(t):
        lo, hi = win_chunks(t)
        nw = hi - lo
        for i in range(nw):
            for nb in range(2):
                mm(psB[:, nb * 512:(nb + 1) * 512], pt_s[i][:],
                   v_s[lo + i][:, nb * 512:(nb + 1) * 512],
                   start=(i == 0), stop=(i == nw - 1),
                   reads=[f"pt{i}", f"v{lo + i}"],
                   writes=["psB.0", "psB.1"] if (i == 0 and nb == 0) else
                          [f"psB.{nb}"],
                   inc=(i == nw - 1 and nb == 1))

    def ln_chain(vkey, vbuf, out_sb, out_keys, extra_reads=(), extra_id_waits=()):
        S.emit("vector", lambda e, o=bst_s[:, 0:6], i=vbuf[:, 0:512]: e.bn_stats(o, i),
               reads=[vkey], writes=["bst.0"])
        S.emit("vector", lambda e, o=bst_s[:, 6:12], i=vbuf[:, 512:1024]:
               e.bn_stats(o, i), reads=[vkey], writes=["bst.1"])
        S.emit("vector", lambda e, o=mv_s[:], i=bst_s[:]: e.bn_aggr(o, i),
               reads=["bst.0", "bst.1"], writes=["mv"])
        lnv, rstd, nmr = s1_s[:, 10:11], s1_s[:, 11:12], s1_s[:, 12:13]
        vep = s1_s[:, 13:14]
        S.emit("vector", lambda e, o=vep, i=mv_s[:, 1:2]:
               e.tensor_scalar(o, i, EPS, None, op0=ALU.add),
               reads=["mv"], writes=["s1.13"])
        S.emit("scalar", lambda e, o=lnv, i=vep: e.activation(o, i, AF.Ln),
               reads=["s1.13"], writes=["s1.10"])
        S.emit("scalar", lambda e, o=rstd, i=lnv: e.activation(o, i, AF.Exp, scale=-0.5),
               reads=["s1.10"], writes=["s1.11"])
        S.emit("vector", lambda e, o=nmr, i=mv_s[:, 0:1], s=rstd:
               e.scalar_tensor_tensor(o, i, -1.0, s, op0=ALU.mult, op1=ALU.mult),
               reads=["mv", "s1.11"], writes=["s1.12"])
        S.emit("scalar",
               lambda e, o=out_sb[:], i=vbuf[:], b=nmr, s=rstd:
               e.activation(o, i, AF.Identity, bias=b, scale=s),
               reads=[vkey, "s1.11", "s1.12"] + list(extra_reads), writes=out_keys,
               waits=extra_id_waits)

    def ln_x2(t):
        slot = t % 2
        S.emit("vector", lambda e, o=v1_s[:], a=psB[:], b=xr_s[slot][:]:
               e.tensor_add(o, a, b),
               reads=["psB.0", "psB.1", f"xr{slot}"], writes=["v1b"])
        ln_chain("v1b", v1_s, x2k_t, ["x2k_t"])
        dma("sync", "x2kw", x2k_d[t * 128:(t + 1) * 128, :], x2k_t[:],
            reads=["x2k_t"])
        if "dbg_x2" in tap_aps:
            dma("sync", "x2kw", tap_aps["dbg_x2"][t * 128:(t + 1) * 128, :], x2k_t[:],
                reads=["x2k_t"])

    def x2transp(t):
        for c in range(DC):
            S.emit("tensor",
                   lambda e, o=psT[:, c * 128:(c + 1) * 128],
                          x=x2k_t[:, c * 128:(c + 1) * 128], idn=id_s[:]:
                   e.transpose(o, x, idn),
                   reads=["x2k_t", "id"], writes=["psT"], inc=(c == DC - 1))
        for c in range(DC):
            S.emit("scalar",
                   lambda e, o=x2t_t[:, c * 128:(c + 1) * 128],
                          x=psT[:, c * 128:(c + 1) * 128]: e.copy(o, x),
                   reads=["psT"], writes=[f"x2t_t.{c}"])
        dst = x2t_d.rearrange("(c p) n -> p c n", p=128)[:, :, t * 128:(t + 1) * 128]
        dma("sync", "x2tw", dst, x2t_t[:].rearrange("p (c n) -> p c n", n=128),
            reads=[f"x2t_t.{c}" for c in range(DC)])

    tile_dmas(0)
    tile_dmas(1)
    for t in range(NQ + 1):
        if t < NQ:
            qproj(t)
            rot_q(t)
        if t >= 1:
            transp(t - 1)
            pv(t - 1)
            ln_x2(t - 1)
        if t < NQ:
            squarter(t, 0)
            max_q(t, 0)
            exp_q(t, 0)
            squarter(t, 1)
            max_q(t, 1)
            exp_q(t, 1)
        if t >= 1:
            x2transp(t - 1)
        if t < NQ:
            squarter(t, 2)
            max_q(t, 2)
            exp_q(t, 2)
            squarter(t, 3)
            max_q(t, 3)
            exp_q(t, 3)
            softmax_tail(t)
        if 2 <= t + 1 < NQ:
            tile_dmas(t + 1)
    M["phA_T"] = S.mark("T")
    M["phA_V"] = S.mark("V")
    M["phA_S"] = S.mark("S")
    PH_BARA = [("T", M["phA_T"]), ("V", M["phA_V"]), ("S", M["phA_S"]),
               ("tapw", S.mark("tapw")), ("x2kw", S.mark("x2kw")),
               ("x2tw", S.mark("x2tw"))]

    def wg_dma(e_, g, slot, waits=()):
        src1 = w1_d[e_].rearrange("(c p) f -> p c f", p=128)[:, :, g * 1024:(g + 1) * 1024]
        dma("gpsimd", f"w1g{slot}",
            w1g_s[slot][:].rearrange("p (c n) -> p c n", n=1024), src1,
            writes=[f"w1g{slot}"], waits=waits)
        dma("gpsimd", f"w2g{slot}",
            w2g_s[slot][:].rearrange("p (c n) -> p c n", n=1024),
            w2_d[e_].rearrange("(c p) n -> p c n", p=128)[:, g * 8:(g + 1) * 8, :],
            writes=[f"w2g{slot}"])

    gseq = [(e_, g) for e_ in range(NE) for g in range(4)]
    NGI = len(gseq)

    def x2ts_load(s_):
        sl = s_ % 2
        dma("sync", f"x2tsl{sl}", x2ts_s[sl][:].rearrange("p (c n) -> p c n", n=SUP),
            x2t_d.rearrange("(c p) n -> p c n", p=128)[:, :, s_ * SUP:(s_ + 1) * SUP],
            writes=[f"x2ts{sl}"],
            waits=PH_BARA + [("x2tw", S.mark("x2tw"))])

    def router(s_):
        rpar = s_ % 2
        rx2t = x2ts_s[rpar]
        rcc = c_s[rpar]
        for tcl in range(NT):
            for di in range(DC):
                mm(psC[:, 0:NE],
                   rx2t[:, di * SUP + tcl * 128: di * SUP + tcl * 128 + 128],
                   rwb_s[:, di * NE:(di + 1) * NE],
                   start=(di == 0), stop=(di == DC - 1),
                   reads=[f"x2ts{rpar}", "rwb"], writes=["psC"],
                   inc=(di == DC - 1))
            S.emit("vector", lambda e, o=lg_s[:], a=psC[:, 0:NE], b=rbb_s[:]:
                   e.tensor_add(o, a, b),
                   reads=["psC", "rbb"], writes=["lg"],
                   waits=PH_BARA if s_ == 0 and tcl == 0 else ())
            m1, m2 = s1_s[:, 0:1], s1_s[:, 1:2]
            nd, e2v, den, g1, g2 = (s1_s[:, 2:3], s1_s[:, 3:4], s1_s[:, 4:5],
                                    s1_s[:, 5:6], s1_s[:, 6:7])
            S.emit("vector", lambda e, o=m1, i=lg_s[:]: e.reduce_max(o, i, axis=AX),
                   reads=["lg"], writes=["s1.0"])
            S.emit("vector", lambda e, o=oh1_s[:], i=lg_s[:], s=m1:
                   e.tensor_scalar(o, i, s, None, op0=ALU.is_equal),
                   reads=["lg", "s1.0"], writes=["oh1"])
            S.emit("vector", lambda e, o=msk_s[:], i=oh1_s[:], b=lg_s[:]:
                   e.scalar_tensor_tensor(o, i, -1e30, b, op0=ALU.mult, op1=ALU.add),
                   reads=["oh1", "lg"], writes=["msk"])
            S.emit("vector", lambda e, o=m2, i=msk_s[:]: e.reduce_max(o, i, axis=AX),
                   reads=["msk"], writes=["s1.1"])
            S.emit("vector", lambda e, o=oh2_s[:], i=msk_s[:], s=m2:
                   e.tensor_scalar(o, i, s, None, op0=ALU.is_equal),
                   reads=["msk", "s1.1"], writes=["oh2"])
            S.emit("vector", lambda e, o=nd, a=m2, b=m1: e.tensor_sub(o, a, b),
                   reads=["s1.0", "s1.1"], writes=["s1.2"])
            S.emit("scalar", lambda e, o=e2v, i=nd: e.activation(o, i, AF.Exp),
                   reads=["s1.2"], writes=["s1.3"])
            S.emit("vector", lambda e, o=den, i=e2v:
                   e.tensor_scalar(o, i, 1.0, None, op0=ALU.add),
                   reads=["s1.3"], writes=["s1.4"])
            S.emit("vector", lambda e, o=g1, i=den: e.reciprocal(o, i),
                   reads=["s1.4"], writes=["s1.5"])
            S.emit("vector", lambda e, o=g2, i=g1:
                   e.tensor_scalar(o, i, -1.0, 1.0, op0=ALU.mult, op1=ALU.add),
                   reads=["s1.5"], writes=["s1.6"])
            S.emit("vector", lambda e, o=oh1_s[:], i=oh1_s[:], s=g1:
                   e.tensor_scalar(o, i, s, None, op0=ALU.mult),
                   reads=["oh1", "s1.5"], writes=["oh1"])
            S.emit("vector",
                   lambda e, o=rcc[:, tcl * NE:(tcl + 1) * NE], i=oh2_s[:], s=g2,
                          b=oh1_s[:]:
                   e.scalar_tensor_tensor(o, i, s, b, op0=ALU.mult, op1=ALU.add),
                   reads=["oh2", "s1.6", "oh1"], writes=[f"c{rpar}.{tcl}"])

    def final_ln(s_, tcl, last_gi):
        S.emit("vector",
               lambda e, o=v2_s[:], a=x2ks_s[:, tcl * D:(tcl + 1) * D],
                      b=acc_s[tcl][:]: e.tensor_add(o, a, b),
               reads=["x2ks", f"acc{tcl}"], writes=["v2b"])
        oslot = tcl % 2
        extra = []
        if s_ == 0 and tcl < 2:
            extra = [("x2tw", S.mark("x2tw")), ("x2kw", S.mark("x2kw"))]
        ln_chain("v2b", v2_s, outst_s[oslot], [f"outst{oslot}"], extra_id_waits=extra)
        row = s_ * SUP + tcl * 128
        dma("sync", f"outw{oslot}", out_d[row:row + 128, :], outst_s[oslot][:],
            reads=[f"outst{oslot}"])

    x2ts_load(0)
    for s_ in range(NS):
        par = s_ % 2
        xk = f"x2ts{par}"
        x2t = x2ts_s[par]
        cc = c_s[par]
        if s_ + 1 < NS:
            x2ts_load(s_ + 1)
        dma("sync", "x2ks", x2ks_s[:].rearrange("p (c n) -> p c n", n=D),
            x2k_d.rearrange("(c p) n -> p c n", p=128)[:, s_ * NT:(s_ + 1) * NT, :],
            writes=["x2ks"],
            waits=PH_BARA + [("x2kw", S.mark("x2kw"))])
        if s_ == 0:
            router(0)

        for ei, (e_, g) in enumerate(gseq):
            gi = s_ * NGI + ei
            slot = gi % 2
            if ei == NGI // 2 and s_ + 1 < NS:
                router(s_ + 1)
            if gi == 0:
                wg_dma(e_, g, 0, waits=PH_BARA)
                nx = gseq[1]
                wg_dma(nx[0], nx[1], 1)
            if ei + 1 < NGI:
                nx = gseq[ei + 1]
            elif s_ + 1 < NS:
                nx = gseq[0]
            else:
                nx = None
            if gi >= 1 and nx is not None:
                wg_dma(nx[0], nx[1], (gi + 1) % 2)

            if g == 0:
                se = s_ * NE + e_
                dma("sync", f"b2r{e_ % 2}", b2r_s[e_ % 2][0:1, :], b2_d[e_:e_ + 1, :],
                    writes=[f"b2r{e_ % 2}"])
                for nb in range(2):
                    mm(psC[:], ones_s[0:1, :],
                       b2r_s[e_ % 2][0:1, nb * 512:(nb + 1) * 512],
                       start=True, stop=True,
                       reads=["ones", f"b2r{e_ % 2}"], writes=["psC"], inc=True)
                    S.emit("vector",
                           lambda e, o=b2b_s[:, nb * 512:(nb + 1) * 512], i=psC[:]:
                           e.tensor_copy(o, i), reads=["psC"], writes=["b2b"])

            hp = [(psA, "psA"), (psA2, "psA2")]
            for fc in range(8):
                ps, pk = hp[fc % 2]
                for di in range(DC):
                    lhsT = w1g_s[slot][:, di * 1024 + fc * 128: di * 1024 + fc * 128 + 128]
                    for nb in range(2):
                        mm(ps[:, nb * 512:(nb + 1) * 512], lhsT,
                           x2t[:, di * SUP + nb * 512: di * SUP + (nb + 1) * 512],
                           start=(di == 0), stop=(di == DC - 1),
                           reads=[f"w1g{slot}", xk],
                           writes=[f"{pk}.0", f"{pk}.1"] if (di == 0 and nb == 0) else
                                  [f"{pk}.{nb}"],
                           inc=(di == DC - 1 and nb == 1))
                bcol = b1_s[:, e_ * FC + g * 8 + fc: e_ * FC + g * 8 + fc + 1]
                S.emit("scalar", lambda e, o=h_s[fc][:], i=ps[:], b=bcol:
                       e.activation(o, i, AF.Relu, bias=b),
                       reads=[f"{pk}.0", f"{pk}.1", "b1"], writes=[f"h{fc}"])

            for tcl in range(8):
                for nb in range(2):
                    for fc in range(8):
                        mm(psB[:, nb * 512:(nb + 1) * 512],
                           h_s[fc][:, tcl * 128:(tcl + 1) * 128],
                           w2g_s[slot][:, fc * 1024 + nb * 512: fc * 1024 + (nb + 1) * 512],
                           start=(fc == 0), stop=(fc == 7),
                           reads=[f"h{fc}", f"w2g{slot}"], writes=[f"psB.{nb}"],
                           inc=(fc == 7))
                ccol = cc[:, tcl * NE + e_: tcl * NE + e_ + 1]
                if g == 0 and e_ == 0:
                    S.emit("vector", lambda e, o=acc_s[tcl][:], i=b2b_s[:], s=ccol:
                           e.tensor_scalar(o, i, s, None, op0=ALU.mult),
                           reads=["b2b", f"c{par}.{tcl}"], writes=[f"acc{tcl}"],
                           waits=PH_BARA if s_ == 0 else ())
                elif g == 0:
                    S.emit("vector", lambda e, o=acc_s[tcl][:], i=b2b_s[:], s=ccol:
                           e.scalar_tensor_tensor(o, i, s, o, op0=ALU.mult, op1=ALU.add),
                           reads=["b2b", f"c{par}.{tcl}", f"acc{tcl}"],
                           writes=[f"acc{tcl}"])
                for nb in range(2):
                    S.emit("vector",
                           lambda e, o=acc_s[tcl][:, nb * 512:(nb + 1) * 512],
                                  i=psB[:, nb * 512:(nb + 1) * 512], s=ccol:
                           e.scalar_tensor_tensor(o, i, s, o, op0=ALU.mult, op1=ALU.add),
                           reads=[f"psB.{nb}", f"c{par}.{tcl}", f"acc{tcl}"],
                           writes=[f"acc{tcl}"])
                if ei == NGI - 1:
                    final_ln(s_, tcl, gi)

    S.emit("sync", lambda e: e.nop(),
           waits=[(s, S.mark(s)) for s in
                  ("outw0", "outw1", "tapw", "x2kw", "x2tw") if S.mark(s)])

    names = set(S.cnt)
    for eng in Sched.ENGINES:
        for ws, _, _ in S.steps[eng]:
            names.update(s for s, _ in ws)
    with contextlib.ExitStack() as stack:
        sems = {s: stack.enter_context(nc.semaphore(f"sem_{s}")) for s in sorted(names)}
        block = stack.enter_context(nc.Block())
        for eng in Sched.ENGINES:
            def mk_section(steps):
                def _(e):
                    for ws, fn, incs in steps:
                        for s, v in ws:
                            e.wait_ge(sems[s], v)
                        inst = fn(e)
                        for s, v in incs:
                            inst.then_inc(sems[s], v)
                return _
            getattr(block, eng)(mk_section(S.steps[eng]))
    return nc


def kernel(L0_attn_X, L0_attn_W_Q, L0_attn_W_K, L0_attn_W_V,
           L0_ffn_router_w, L0_ffn_router_b, L0_ffn_W1, L0_ffn_b1,
           L0_ffn_W2, L0_ffn_b2, _taps=(), _tap_cb=None):
    cq, sq, ck, sk, g, ident, onesf, perm = _constants()
    f32 = np.float32
    wq_p = np.ascontiguousarray(np.asarray(L0_attn_W_Q, f32)[:, perm])
    wk_p = np.ascontiguousarray(np.asarray(L0_attn_W_K, f32)[:, perm])
    wv = np.ascontiguousarray(np.asarray(L0_attn_W_V, f32))
    rbb = np.ascontiguousarray(np.broadcast_to(
        np.asarray(L0_ffn_router_b, f32)[None, :], (128, NE)))
    w1 = np.ascontiguousarray(np.asarray(L0_ffn_W1, f32))
    w2 = np.ascontiguousarray(np.asarray(L0_ffn_W2, f32))
    b1 = np.ascontiguousarray(
        np.asarray(L0_ffn_b1, f32).reshape(NE, FC, 128).transpose(2, 0, 1)
        .reshape(128, NE * FC))
    b2 = np.ascontiguousarray(np.asarray(L0_ffn_b2, f32))
    rw = np.ascontiguousarray(np.asarray(L0_ffn_router_w, f32))
    X = np.asarray(L0_attn_X, f32)

    nc = build(taps=_taps)
    in_maps = []
    for b in range(8):
        xb = np.ascontiguousarray(X[b])
        in_maps.append({
            "xt": np.ascontiguousarray(xb.T), "xtok": xb,
            "wq": wq_p, "wk": wk_p, "wv": wv,
            "cq": cq, "sq": sq, "ck": ck, "sk": sk, "gband": g,
            "ident": ident, "onesf": onesf, "rw": rw, "rbb": rbb,
            "w1": w1, "b1": b1, "w2": w2, "b2": b2,
        })
    res = run_bass_kernel_spmd(nc, in_maps, core_ids=list(range(8)))
    if _tap_cb is not None:
        _tap_cb(res)
    return np.stack([res.results[b]["out"] for b in range(8)], axis=0)


# version 08
# speedup vs baseline: 1.1005x; 1.1005x over previous
import contextlib
import numpy as np
import ml_dtypes

import concourse.bass as bass
import concourse.mybir as mybir
from concourse.bass_utils import run_bass_kernel_spmd

F32 = mybir.dt.float32
BF16 = mybir.dt.bfloat16
AF = mybir.ActivationFunctionType
ALU = mybir.AluOpType
AX = mybir.AxisListType.X

L = 4096
D = 1024
DC = 8
TC = 32
NQ = 32
DFF = 4096
FC = 32
NE = 8
BAND = 384
GW = 2 * BAND + 128
SUP = 1024
NS = L // SUP
NT = SUP // 128
GAMMA = 0.9
EPS = 1e-5

ENG_SEMS = {"tensor": "T", "vector": "V", "scalar": "S"}


class _Sentinel:
    def __repr__(self):
        return "PENDING"


PENDING = _Sentinel()


class Sched:
    ENGINES = ("tensor", "vector", "scalar", "gpsimd", "sync")

    def __init__(self):
        self.steps = {e: [] for e in self.ENGINES}
        self.cnt = {}
        self.own = {}
        self.readers = {}
        self.seen = {e: {} for e in self.ENGINES}
        self.pending_t = []

    def mark(self, sem):
        return self.cnt.get(sem, 0)

    def emit(self, eng, fn, reads=(), writes=(), waits=(), inc=False, dma_sem=None):
        need = {}

        def add(pos):
            s, v = pos[0], pos[1]
            if not v:
                return
            if eng == "tensor" and s == "T":
                return
            assert v is not PENDING, "edge to unresolved tensor effect"
            if v > need.get(s, 0):
                need[s] = v

        for s, v in waits:
            add((s, v))
        for k in reads:
            o = self.own.get(k)
            if o:
                add(o)
        for k in writes:
            o = self.own.get(k)
            if o:
                add(o)
            for p in self.readers.get(k, {}).values():
                add(p)

        ws = []
        for s, v in sorted(need.items()):
            have = self.cnt.get(s, 0)
            assert v <= have, f"wait {s}>={v} but only {have} emitted"
            if s not in ("T", "V", "S"):
                assert v == have, f"prefix wait on DMA sem {s}: {v} < {have}"
            if self.seen[eng].get(s, 0) >= v:
                continue
            self.seen[eng][s] = v
            ws.append((s, v))

        incs = ()
        if dma_sem is not None:
            self.cnt[dma_sem] = self.cnt.get(dma_sem, 0) + 16
            incs = ((dma_sem, 16),)
            pos_ref = [dma_sem, self.cnt[dma_sem]]
        elif eng == "tensor":
            if inc:
                self.cnt["T"] = self.cnt.get("T", 0) + 1
                incs = (("T", 1),)
                mk = self.cnt["T"]
                for rec in self.pending_t:
                    rec[1] = mk
                self.pending_t = []
                pos_ref = ["T", mk]
            else:
                pos_ref = ["T", PENDING]
                self.pending_t.append(pos_ref)
        elif eng in ENG_SEMS:
            sem = ENG_SEMS[eng]
            self.cnt[sem] = self.cnt.get(sem, 0) + 1
            incs = ((sem, 1),)
            pos_ref = [sem, self.cnt[sem]]
        else:
            pos_ref = ["", 0]

        self.steps[eng].append((tuple(ws), fn, incs))

        for k in writes:
            self.own[k] = pos_ref
            self.readers[k] = {}
        for k in reads:
            self.readers.setdefault(k, {})[id(pos_ref)] = pos_ref
        return self.cnt.get(dma_sem if dma_sem is not None else
                            ENG_SEMS.get(eng, ""), 0)


def _constants():
    i = np.arange(512, dtype=np.float64)
    scale_vec = (2 * i + 0.4 * D) / (1.4 * D)
    inv_freq = 1.0 / 10000.0 ** (i / 512.0)
    n = np.arange(L, dtype=np.float64)
    scale = scale_vec[None, :] ** (n[:, None] / 512.0)
    sinu = n[:, None] * inv_freq[None, :]
    cq = np.ascontiguousarray((np.cos(sinu) * scale).T.astype(np.float32))
    sq = np.ascontiguousarray((np.sin(sinu) * scale).T.astype(np.float32))
    ck = np.ascontiguousarray((np.cos(sinu) / scale).T.astype(np.float32))
    sk = np.ascontiguousarray((np.sin(sinu) / scale).T.astype(np.float32))
    ii = np.arange(128, dtype=np.float64)[:, None]
    jj = np.arange(GW, dtype=np.float64)[None, :]
    g = (GAMMA ** np.abs(BAND + ii - jj)).astype(np.float32)
    ident = np.eye(128, dtype=ml_dtypes.bfloat16)
    onesf = np.ones((1, 128), dtype=np.float32)
    perm = np.concatenate([np.arange(0, D, 2), np.arange(1, D, 2)])
    return cq, sq, ck, sk, g, ident, onesf, perm


def win_chunks(t):
    return max(0, t - BAND // 128), min(TC, t + 1 + BAND // 128)


def build(taps=()):
    nc = bass.Bass("TRN2", target_bir_lowering=False)

    xt = nc.dram_tensor("xt", [D, L], F32, kind="ExternalInput").ap()
    xtok = nc.dram_tensor("xtok", [L, D], F32, kind="ExternalInput").ap()
    wq_d = nc.dram_tensor("wq", [D, D], F32, kind="ExternalInput").ap()
    wk_d = nc.dram_tensor("wk", [D, D], F32, kind="ExternalInput").ap()
    wv_d = nc.dram_tensor("wv", [D, D], F32, kind="ExternalInput").ap()
    cq_d = nc.dram_tensor("cq", [512, L], F32, kind="ExternalInput").ap()
    sq_d = nc.dram_tensor("sq", [512, L], F32, kind="ExternalInput").ap()
    ck_d = nc.dram_tensor("ck", [512, L], F32, kind="ExternalInput").ap()
    sk_d = nc.dram_tensor("sk", [512, L], F32, kind="ExternalInput").ap()
    g_d = nc.dram_tensor("gband", [128, GW], F32, kind="ExternalInput").ap()
    id_d = nc.dram_tensor("ident", [128, 128], BF16, kind="ExternalInput").ap()
    ones_d = nc.dram_tensor("onesf", [1, 128], F32, kind="ExternalInput").ap()
    rw_d = nc.dram_tensor("rw", [D, NE], F32, kind="ExternalInput").ap()
    rbb_d = nc.dram_tensor("rbb", [128, NE], F32, kind="ExternalInput").ap()
    w1_d = nc.dram_tensor("w1", [NE, D, DFF], F32, kind="ExternalInput").ap()
    b1_d = nc.dram_tensor("b1", [128, NE * FC], F32, kind="ExternalInput").ap()
    w2_d = nc.dram_tensor("w2", [NE, DFF, D], F32, kind="ExternalInput").ap()
    b2_d = nc.dram_tensor("b2", [NE, D], F32, kind="ExternalInput").ap()
    out_d = nc.dram_tensor("out", [L, D], F32, kind="ExternalOutput").ap()
    x2t_d = nc.dram_tensor("x2t_s", [D, L], BF16, kind="Internal").ap()
    x2k_d = nc.dram_tensor("x2k_s", [L, D], BF16, kind="Internal").ap()

    tap_aps = {}
    for name, shape, dt in taps:
        tap_aps[name] = nc.dram_tensor(name, shape, dt, kind="ExternalOutput").ap()

    SB_BASE = 18432
    SB_LIMIT = 229376 - 256

    def nbytes(shape, dtype):
        return int(np.prod(shape[1:])) * (2 if dtype == BF16 else 4)

    class Bump:
        def __init__(self, base, limit=SB_LIMIT):
            self.off = base
            self.limit = limit

        def sb(self, name, shape, dtype):
            self.off = (self.off + 31) & ~31
            t = nc.alloc_sbuf_tensor_at(name, shape, dtype, offset=self.off)
            self.off += nbytes(shape, dtype)
            assert self.off <= self.limit, (name, self.off)
            return t

    P = Bump(SB_BASE)
    g_s = P.sb("g_s", [128, GW], F32)
    id_s = P.sb("id_s", [128, 128], BF16)
    ones_s = P.sb("ones_s", [1, 128], F32)
    rbb_s = P.sb("rbb_s", [128, NE], F32)
    rwb_s = P.sb("rwb_s", [128, DC * NE], BF16)
    b1_s = P.sb("b1_s", [128, NE * FC], F32)
    qf_s = P.sb("qf_s", [128, D], F32)
    rsc_s = P.sb("rsc_s", [128, 512], F32)
    mv_s = P.sb("mv_s", [128, 2], F32)
    bst_s = P.sb("bst_s", [128, 12], F32)
    s1_s = P.sb("s1_s", [128, 16], F32)
    zq_s = P.sb("zq_s", [128, 4], F32)
    mq_s = P.sb("mq_s", [128, 4], F32)
    dq_s = P.sb("dq_s", [128, 4], F32)
    eq_s = P.sb("eq_s", [128, 4], F32)
    rq_s = P.sb("rq_s", [128, 4], F32)
    PERSIST_END = P.off

    A = Bump(PERSIST_END)
    kt_s = [A.sb(f"kt{c}", [128, L], BF16) for c in range(DC)]
    v_s = [A.sb(f"v{c}", [128, D], BF16) for c in range(TC)]
    PH0A_END = A.off

    Z = Bump(PH0A_END)
    w0_s = Z.sb("w0_s", [128, 2 * DC * D], BF16)
    xsl_s = [Z.sb(f"xsl{i}", [128, DC * 512], BF16) for i in range(2)]
    kf_s = Z.sb("kf_s", [128, 2048], F32)
    cs_s = [Z.sb(f"cs{i}", [128, 1024], F32) for i in range(2)]

    Z = Bump(PH0A_END)
    wqb_s = Z.sb("wqb_s", [128, DC * D], BF16)
    xq_s = [Z.sb(f"xq{i}", [128, D], BF16) for i in range(2)]
    cqs_s = [Z.sb(f"cqs{i}", [128, 512], F32) for i in range(2)]
    sqs_s = [Z.sb(f"sqs{i}", [128, 512], F32) for i in range(2)]
    xr_s = [Z.sb(f"xr{i}", [128, D], F32) for i in range(2)]
    exp_s = Z.sb("exp_s", [128, L], BF16)
    pb_s = Z.sb("pb_s", [128, GW], BF16)
    qt_s = Z.sb("qt_s", [128, D], BF16)
    pt_s = [Z.sb(f"pt{i}", [128, 128], BF16) for i in range(7)]
    v1_s = Z.sb("v1_s", [128, D], F32)
    x2k_t = Z.sb("x2k_t", [128, D], BF16)
    x2t_t = Z.sb("x2t_t", [128, D], BF16)

    Z = Bump(PERSIST_END)
    x2ts_s = [Z.sb(f"x2ts{i}", [128, DC * SUP], BF16) for i in range(2)]
    x2ks_s = Z.sb("x2ks_s", [128, NT * D], BF16)
    h_s = [Z.sb(f"h{i}", [128, SUP], BF16) for i in range(8)]
    acc_s = [Z.sb(f"acc{i}", [128, D], F32) for i in range(NT)]
    b2b_s = Z.sb("b2b_s", [128, D], F32)
    b2r_s = [Z.sb(f"b2r{i}", [1, D], F32) for i in range(2)]
    v2_s = Z.sb("v2_s", [128, D], F32)
    lg_s = Z.sb("lg_s", [128, NE], F32)
    oh1_s = Z.sb("oh1_s", [128, NE], F32)
    oh2_s = Z.sb("oh2_s", [128, NE], F32)
    msk_s = Z.sb("msk_s", [128, NE], F32)
    c_s = [Z.sb(f"c_s{i}", [128, NT * NE], F32) for i in range(2)]
    outst_s = [Z.sb(f"outst{i}", [128, D], F32) for i in range(2)]
    w1g_s = [Z.sb(f"w1g{i}", [128, 8 * 1024], BF16) for i in range(2)]
    w2g_s = [Z.sb(f"w2g{i}", [128, 8 * 1024], BF16) for i in range(2)]

    psA = nc.alloc_psum_tensor("psA", [128, 1024], F32)
    psA2 = nc.alloc_psum_tensor("psA2", [128, 1024], F32)
    psB = nc.alloc_psum_tensor("psB", [128, 1024], F32)
    psT = nc.alloc_psum_tensor("psT", [128, 1024], BF16)
    psC = nc.alloc_psum_tensor("psC", [128, 512], F32)

    S = Sched()
    M = {}

    def dma(eng, sem, out, in_, reads=(), writes=(), waits=()):
        return S.emit(eng, lambda e, o=out, i=in_: e.dma_start(out=o, in_=i),
                      reads=reads, writes=writes, waits=waits, dma_sem=sem)

    def mm(out, lhsT, rhs, start, stop, reads=(), writes=(), waits=(), inc=False):
        return S.emit("tensor",
                      lambda e, o=out, l=lhsT, r=rhs, st=start, sp=stop:
                      e.matmul(o, l, r, start=st, stop=sp),
                      reads=reads, writes=writes, waits=waits, inc=inc)

    dma("sync", "pre_g", g_s[:], g_d, writes=["g"])
    dma("sync", "pre_id", id_s[:], id_d, writes=["id"])
    dma("sync", "pre_on", ones_s[:], ones_d, writes=["ones"])
    dma("sync", "pre_rb", rbb_s[:], rbb_d, writes=["rbb"])
    dma("sync", "pre_b1", b1_s[:], b1_d, writes=["b1"])
    dma("gpsimd", "rwb", rwb_s[:].rearrange("p (c e) -> p c e", e=NE),
        rw_d.rearrange("(c p) e -> p c e", p=128), writes=["rwb"])
    dma("gpsimd", "gw", w0_s[:, 0:DC * D].rearrange("p (c n) -> p c n", n=D),
        wk_d.rearrange("(c p) n -> p c n", p=128), writes=["w0"])
    dma("gpsimd", "gwv", w0_s[:, DC * D:2 * DC * D].rearrange("p (c n) -> p c n", n=D),
        wv_d.rearrange("(c p) n -> p c n", p=128), writes=["w0v"])

    def xslab_dma(slot, tb):
        src = xt.rearrange("(c p) n -> p c n", p=128)[:, :, tb * 512:(tb + 1) * 512]
        dst = xsl_s[slot][:].rearrange("p (c n) -> p c n", n=512)
        return dma("gpsimd", f"xsl{slot}", dst, src, writes=[f"xsl{slot}"])

    for tb in range(8):
        xslab_dma(tb % 2, tb)
        for pr in range(4):
            idx = tb * 4 + pr
            csl = idx % 2
            dma("sync", f"cs{csl}", cs_s[csl][:, 0:512],
                ck_d[pr * 128:(pr + 1) * 128, tb * 512:(tb + 1) * 512],
                writes=[f"cs{csl}"])
            dma("sync", f"cs{csl}", cs_s[csl][:, 512:1024],
                sk_d[pr * 128:(pr + 1) * 128, tb * 512:(tb + 1) * 512],
                writes=[f"cs{csl}"])
            for half, do in ((0, pr), (1, pr + 4)):
                for di in range(DC):
                    mm(psA[:, half * 512:(half + 1) * 512],
                       w0_s[:, di * D + do * 128: di * D + do * 128 + 128],
                       xsl_s[tb % 2][:, di * 512:(di + 1) * 512],
                       start=(di == 0), stop=(di == DC - 1),
                       reads=["w0", f"xsl{tb % 2}"], writes=[f"psA.{half}"],
                       inc=(di == DC - 1))
            for half in range(2):
                S.emit("scalar",
                       lambda e, o=kf_s[:, half * 512:(half + 1) * 512],
                              i=psA[:, half * 512:(half + 1) * 512]: e.copy(o, i),
                       reads=[f"psA.{half}"], writes=[f"kf.e{half}"])
            tc = idx
            tc4 = pr
            pk = "psA2" if tc % 2 == 0 else "psB"
            ps = psA2 if tc % 2 == 0 else psB
            for di in range(DC):
                lhsT = xsl_s[tb % 2][:, di * 512 + tc4 * 128: di * 512 + tc4 * 128 + 128]
                for nb in range(2):
                    mm(ps[:, nb * 512:(nb + 1) * 512], lhsT,
                       w0_s[:, DC * D + di * D + nb * 512:
                            DC * D + di * D + (nb + 1) * 512],
                       start=(di == 0), stop=(di == DC - 1),
                       reads=["w0v", f"xsl{tb % 2}"],
                       writes=[f"{pk}.0", f"{pk}.1"] if nb == 0 and di == 0 else
                              [f"{pk}.{nb}"],
                       inc=(di == DC - 1 and nb == 1))
            S.emit("vector", lambda e, o=v_s[tc][:], i=ps[:]: e.tensor_copy(o, i),
                   reads=[f"{pk}.0", f"{pk}.1"], writes=[f"v{tc}"])
            ke, ko = kf_s[:, 0:512], kf_s[:, 512:1024]
            t1, t2 = kf_s[:, 1024:1536], kf_s[:, 1536:2048]
            Ct, St = cs_s[csl][:, 0:512], cs_s[csl][:, 512:1024]
            colr = slice(tb * 512, (tb + 1) * 512)
            S.emit("vector", lambda e, o=t1, a=ke, b=Ct: e.tensor_mul(o, a, b),
                   reads=["kf.e0", f"cs{csl}"], writes=["kf.t1"])
            S.emit("vector", lambda e, o=t2, a=ko, b=St: e.tensor_mul(o, a, b),
                   reads=["kf.e1", f"cs{csl}"], writes=["kf.t2"])
            S.emit("vector", lambda e, o=kt_s[pr][:, colr], a=t1, b=t2:
                   e.tensor_sub(o, a, b),
                   reads=["kf.t1", "kf.t2"], writes=[f"kt{pr}"])
            S.emit("vector", lambda e, o=t1, a=ko, b=Ct: e.tensor_mul(o, a, b),
                   reads=["kf.e1", f"cs{csl}"], writes=["kf.t1"])
            S.emit("vector", lambda e, o=t2, a=ke, b=St: e.tensor_mul(o, a, b),
                   reads=["kf.e0", f"cs{csl}"], writes=["kf.t2"])
            S.emit("vector", lambda e, o=kt_s[pr + 4][:, colr], a=t1, b=t2:
                   e.tensor_add(o, a, b),
                   reads=["kf.t1", "kf.t2"], writes=[f"kt{pr + 4}"])
    M["ph0_T"] = S.mark("T")
    M["ph0_V"] = S.mark("V")
    M["ph0_S"] = S.mark("S")
    M["k_done_V"] = S.mark("V")
    PH_BAR0 = [("T", M["ph0_T"]), ("V", M["ph0_V"]), ("S", M["ph0_S"])]

    if "dbg_kt" in tap_aps:
        for c in range(DC):
            dma("sync", "tapw", tap_aps["dbg_kt"][c * 128:(c + 1) * 128, :], kt_s[c][:],
                reads=[f"kt{c}"])
    if "dbg_v" in tap_aps:
        for tc in range(TC):
            dma("sync", "tapw", tap_aps["dbg_v"][tc * 128:(tc + 1) * 128, :], v_s[tc][:],
                reads=[f"v{tc}"])

    dma("gpsimd", "gw", wqb_s[:].rearrange("p (c n) -> p c n", n=D),
        wq_d.rearrange("(c p) n -> p c n", p=128), writes=["wqb"],
        waits=PH_BAR0)

    def tile_dmas(t):
        slot = t % 2
        bar = PH_BAR0 if t < 2 else ()
        src = xt.rearrange("(c p) n -> p c n", p=128)[:, :, t * 128:(t + 1) * 128]
        dma("gpsimd", f"xq{slot}", xq_s[slot][:].rearrange("p (c n) -> p c n", n=128),
            src, writes=[f"xq{slot}"], waits=bar)
        dma("sync", f"cqs{slot}", cqs_s[slot][:].rearrange("p (c n) -> p c n", n=128),
            cq_d.rearrange("(c p) n -> p c n", p=128)[:, :, t * 128:(t + 1) * 128],
            writes=[f"cqs{slot}"], waits=bar)
        dma("sync", f"cqs{slot}", sqs_s[slot][:].rearrange("p (c n) -> p c n", n=128),
            sq_d.rearrange("(c p) n -> p c n", p=128)[:, :, t * 128:(t + 1) * 128],
            writes=[f"cqs{slot}"])
        dma("sync", f"xr{slot}", xr_s[slot][:], xtok[t * 128:(t + 1) * 128, :],
            writes=[f"xr{slot}"], waits=bar)

    def qproj(t):
        slot = t % 2
        for do in range(DC):
            for di in range(DC):
                mm(psB[:, do * 128:(do + 1) * 128],
                   wqb_s[:, di * D + do * 128: di * D + do * 128 + 128],
                   xq_s[slot][:, di * 128:(di + 1) * 128],
                   start=(di == 0), stop=(di == DC - 1),
                   reads=["wqb", f"xq{slot}"],
                   writes=["psB.0", "psB.1"] if (do == 0 and di == 0) else
                          [f"psB.{do // 4}"],
                   inc=(di == DC - 1))

    def qf_evac(t):
        for do in range(DC):
            if do < 4:
                S.emit("scalar",
                       lambda e, o=qf_s[:, do * 128:(do + 1) * 128],
                              i=psB[:, do * 128:(do + 1) * 128]: e.copy(o, i),
                       reads=[f"psB.{do // 4}"], writes=[f"qf.{do}"])
            else:
                S.emit("vector",
                       lambda e, o=qf_s[:, do * 128:(do + 1) * 128],
                              i=psB[:, do * 128:(do + 1) * 128]: e.tensor_copy(o, i),
                       reads=[f"psB.{do // 4}"], writes=[f"qf.{do}"])

    def rot_q(t):
        slot = t % 2
        for pr in range(4):
            qe = qf_s[:, pr * 128:(pr + 1) * 128]
            qo = qf_s[:, (pr + 4) * 128:(pr + 5) * 128]
            Ct = cqs_s[slot][:, pr * 128:(pr + 1) * 128]
            St = sqs_s[slot][:, pr * 128:(pr + 1) * 128]
            t1, t2 = rsc_s[:, 0:128], rsc_s[:, 128:256]
            t3, t4 = rsc_s[:, 256:384], rsc_s[:, 384:512]
            S.emit("vector", lambda e, o=t1, a=qe, b=Ct: e.tensor_mul(o, a, b),
                   reads=[f"qf.{pr}", f"cqs{slot}"], writes=["rsc.1"],
                   waits=PH_BAR0 if t == 0 else ())
            S.emit("vector", lambda e, o=t2, a=qo, b=St: e.tensor_mul(o, a, b),
                   reads=[f"qf.{pr + 4}", f"cqs{slot}"], writes=["rsc.2"])
            S.emit("vector", lambda e, o=qt_s[:, pr * 128:(pr + 1) * 128], a=t1, b=t2:
                   e.tensor_sub(o, a, b), reads=["rsc.1", "rsc.2"], writes=[f"qt.{pr}"])
            S.emit("vector", lambda e, o=t3, a=qo, b=Ct: e.tensor_mul(o, a, b),
                   reads=[f"qf.{pr + 4}", f"cqs{slot}"], writes=["rsc.3"])
            S.emit("vector", lambda e, o=t4, a=qe, b=St: e.tensor_mul(o, a, b),
                   reads=[f"qf.{pr}", f"cqs{slot}"], writes=["rsc.4"])
            S.emit("vector", lambda e, o=qt_s[:, (pr + 4) * 128:(pr + 5) * 128],
                          a=t3, b=t4:
                   e.tensor_add(o, a, b), reads=["rsc.3", "rsc.4"],
                   writes=[f"qt.{pr + 4}"])

    def squarter(t, q):
        pk = "psA" if q % 2 == 0 else "psA2"
        ps = psA if q % 2 == 0 else psA2
        for kb in range(2):
            for di in range(DC):
                mm(ps[:, kb * 512:(kb + 1) * 512],
                   qt_s[:, di * 128:(di + 1) * 128],
                   kt_s[di][:, q * 1024 + kb * 512: q * 1024 + (kb + 1) * 512],
                   start=(di == 0), stop=(di == DC - 1),
                   reads=[f"qt.{di}", f"kt{di}"],
                   writes=[f"{pk}.0", f"{pk}.1"] if (kb == 0 and di == 0) else
                          [f"{pk}.{kb}"],
                   inc=(di == DC - 1 and kb == 1))

    def max_q(t, q):
        pk = "psA" if q % 2 == 0 else "psA2"
        ps = psA if q % 2 == 0 else psA2
        S.emit("vector", lambda e, o=mq_s[:, q:q + 1], i=ps[:]:
               e.reduce_max(o, i, axis=AX, negate=True),
               reads=[f"{pk}.0", f"{pk}.1"], writes=[f"mq.{q}"])

    def exp_q(t, q):
        pk = "psA" if q % 2 == 0 else "psA2"
        ps = psA if q % 2 == 0 else psA2
        S.emit("scalar",
               lambda e, o=exp_s[:, q * 1024:(q + 1) * 1024], i=ps[:],
                      b=mq_s[:, q:q + 1], z=zq_s[:, q:q + 1]:
               e.activation(o, i, AF.Exp, bias=b, accum_out=z),
               reads=[f"{pk}.0", f"{pk}.1", f"mq.{q}"],
               writes=[f"exp.{q}", f"zq.{q}"])

    def softmax_tail(t):
        a, b, mn = s1_s[:, 0:1], s1_s[:, 1:2], s1_s[:, 2:3]
        S.emit("vector", lambda e, o=a, x=mq_s[:, 0:1], y=mq_s[:, 1:2]:
               e.tensor_tensor(o, x, y, ALU.min),
               reads=["mq.0", "mq.1"], writes=["s1.0"])
        S.emit("vector", lambda e, o=b, x=mq_s[:, 2:3], y=mq_s[:, 3:4]:
               e.tensor_tensor(o, x, y, ALU.min),
               reads=["mq.2", "mq.3"], writes=["s1.1"])
        S.emit("vector", lambda e, o=mn, x=a, y=b: e.tensor_tensor(o, x, y, ALU.min),
               reads=["s1.0", "s1.1"], writes=["s1.2"])
        for q in range(4):
            S.emit("vector", lambda e, o=dq_s[:, q:q + 1], x=mn, y=mq_s[:, q:q + 1]:
                   e.tensor_sub(o, x, y), reads=["s1.2", f"mq.{q}"], writes=[f"dq.{q}"])
        S.emit("scalar", lambda e, o=eq_s[:], i=dq_s[:]: e.activation(o, i, AF.Exp),
               reads=["dq.0", "dq.1", "dq.2", "dq.3"], writes=["eq"])
        t4, zz, rz = s1_s[:, 4:8], s1_s[:, 8:9], s1_s[:, 9:10]
        S.emit("vector", lambda e, o=t4, x=zq_s[:], y=eq_s[:]: e.tensor_mul(o, x, y),
               reads=["zq.0", "zq.1", "zq.2", "zq.3", "eq"], writes=["s1.47"])
        S.emit("vector", lambda e, o=zz, i=t4: e.reduce_sum(o, i, axis=AX),
               reads=["s1.47"], writes=["s1.8"])
        S.emit("vector", lambda e, o=rz, i=zz: e.reciprocal(o, i),
               reads=["s1.8"], writes=["s1.9"])
        S.emit("vector", lambda e, o=rq_s[:], i=eq_s[:], s=rz:
               e.tensor_scalar(o, i, s, None, op0=ALU.mult),
               reads=["eq", "s1.9"], writes=["rq"])
        lo, hi = win_chunks(t)
        k0 = lo * 128
        goff = BAND - (t * 128 - k0)
        segs = []
        for q in range(4):
            s0 = max(k0, q * 1024)
            s1e = min(hi * 128, (q + 1) * 1024)
            if s0 < s1e:
                segs.append((q, s0, s1e))
        for q, s0, s1e in segs:
            S.emit("vector",
                   lambda e, o=pb_s[:, s0 - k0: s1e - k0],
                          x=exp_s[:, s0:s1e], s=rq_s[:, q:q + 1],
                          gg=g_s[:, goff + (s0 - k0): goff + (s1e - k0)]:
                   e.scalar_tensor_tensor(o, x, s, gg, op0=ALU.mult, op1=ALU.mult),
                   reads=[f"exp.{q}", "rq", "g"], writes=["pb"])

    def transp(t):
        lo, hi = win_chunks(t)
        nw = hi - lo
        for i in range(nw):
            S.emit("tensor",
                   lambda e, o=psT[:, i * 128:(i + 1) * 128],
                          x=pb_s[:, i * 128:(i + 1) * 128], idn=id_s[:]:
                   e.transpose(o, x, idn),
                   reads=["pb", "id"], writes=["psT"], inc=(i == nw - 1))
        for i in range(nw):
            S.emit("scalar",
                   lambda e, o=pt_s[i][:], x=psT[:, i * 128:(i + 1) * 128]: e.copy(o, x),
                   reads=["psT"], writes=[f"pt{i}"])

    def pv(t):
        lo, hi = win_chunks(t)
        nw = hi - lo
        for i in range(nw):
            for nb in range(2):
                mm(psB[:, nb * 512:(nb + 1) * 512], pt_s[i][:],
                   v_s[lo + i][:, nb * 512:(nb + 1) * 512],
                   start=(i == 0), stop=(i == nw - 1),
                   reads=[f"pt{i}", f"v{lo + i}"],
                   writes=["psB.0", "psB.1"] if (i == 0 and nb == 0) else
                          [f"psB.{nb}"],
                   inc=(i == nw - 1 and nb == 1))

    def ln_chain(vkey, vbuf, out_sb, out_keys, extra_reads=(), extra_id_waits=()):
        S.emit("vector", lambda e, o=bst_s[:, 0:6], i=vbuf[:, 0:512]: e.bn_stats(o, i),
               reads=[vkey], writes=["bst.0"])
        S.emit("vector", lambda e, o=bst_s[:, 6:12], i=vbuf[:, 512:1024]:
               e.bn_stats(o, i), reads=[vkey], writes=["bst.1"])
        S.emit("vector", lambda e, o=mv_s[:], i=bst_s[:]: e.bn_aggr(o, i),
               reads=["bst.0", "bst.1"], writes=["mv"])
        lnv, rstd, nmr = s1_s[:, 10:11], s1_s[:, 11:12], s1_s[:, 12:13]
        vep = s1_s[:, 13:14]
        S.emit("vector", lambda e, o=vep, i=mv_s[:, 1:2]:
               e.tensor_scalar(o, i, EPS, None, op0=ALU.add),
               reads=["mv"], writes=["s1.13"])
        S.emit("scalar", lambda e, o=lnv, i=vep: e.activation(o, i, AF.Ln),
               reads=["s1.13"], writes=["s1.10"])
        S.emit("scalar", lambda e, o=rstd, i=lnv: e.activation(o, i, AF.Exp, scale=-0.5),
               reads=["s1.10"], writes=["s1.11"])
        S.emit("vector", lambda e, o=nmr, i=mv_s[:, 0:1], s=rstd:
               e.scalar_tensor_tensor(o, i, -1.0, s, op0=ALU.mult, op1=ALU.mult),
               reads=["mv", "s1.11"], writes=["s1.12"])
        S.emit("scalar",
               lambda e, o=out_sb[:], i=vbuf[:], b=nmr, s=rstd:
               e.activation(o, i, AF.Identity, bias=b, scale=s),
               reads=[vkey, "s1.11", "s1.12"] + list(extra_reads), writes=out_keys,
               waits=extra_id_waits)

    def ln_x2(t):
        slot = t % 2
        S.emit("vector", lambda e, o=v1_s[:], a=psB[:], b=xr_s[slot][:]:
               e.tensor_add(o, a, b),
               reads=["psB.0", "psB.1", f"xr{slot}"], writes=["v1b"])
        ln_chain("v1b", v1_s, x2k_t, ["x2k_t"])
        dma("sync", "x2kw", x2k_d[t * 128:(t + 1) * 128, :], x2k_t[:],
            reads=["x2k_t"])
        if "dbg_x2" in tap_aps:
            dma("sync", "x2kw", tap_aps["dbg_x2"][t * 128:(t + 1) * 128, :], x2k_t[:],
                reads=["x2k_t"])

    def x2transp(t):
        for c in range(DC):
            S.emit("tensor",
                   lambda e, o=psT[:, c * 128:(c + 1) * 128],
                          x=x2k_t[:, c * 128:(c + 1) * 128], idn=id_s[:]:
                   e.transpose(o, x, idn),
                   reads=["x2k_t", "id"], writes=["psT"], inc=(c == DC - 1))
        for c in range(DC):
            S.emit("scalar",
                   lambda e, o=x2t_t[:, c * 128:(c + 1) * 128],
                          x=psT[:, c * 128:(c + 1) * 128]: e.copy(o, x),
                   reads=["psT"], writes=[f"x2t_t.{c}"])
        dst = x2t_d.rearrange("(c p) n -> p c n", p=128)[:, :, t * 128:(t + 1) * 128]
        dma("sync", "x2tw", dst, x2t_t[:].rearrange("p (c n) -> p c n", n=128),
            reads=[f"x2t_t.{c}" for c in range(DC)])

    tile_dmas(0)
    tile_dmas(1)
    for t in range(NQ + 1):
        if t < NQ:
            qproj(t)
            qf_evac(t)
            rot_q(t)
        if t >= 1:
            transp(t - 1)
            pv(t - 1)
            ln_x2(t - 1)
        if t < NQ:
            squarter(t, 0)
            max_q(t, 0)
            exp_q(t, 0)
            squarter(t, 1)
            max_q(t, 1)
            exp_q(t, 1)
        if t >= 1:
            x2transp(t - 1)
        if t < NQ:
            squarter(t, 2)
            max_q(t, 2)
            exp_q(t, 2)
            squarter(t, 3)
            max_q(t, 3)
            exp_q(t, 3)
            softmax_tail(t)
        if 2 <= t + 1 < NQ:
            tile_dmas(t + 1)
    M["phA_T"] = S.mark("T")
    M["phA_V"] = S.mark("V")
    M["phA_S"] = S.mark("S")
    PH_BARA = [("T", M["phA_T"]), ("V", M["phA_V"]), ("S", M["phA_S"]),
               ("tapw", S.mark("tapw")), ("x2kw", S.mark("x2kw")),
               ("x2tw", S.mark("x2tw"))]

    def wg_dma(e_, g, slot, waits=()):
        src1 = w1_d[e_].rearrange("(c p) f -> p c f", p=128)[:, :, g * 1024:(g + 1) * 1024]
        dma("gpsimd", f"w1g{slot}",
            w1g_s[slot][:].rearrange("p (c n) -> p c n", n=1024), src1,
            writes=[f"w1g{slot}"], waits=waits)
        dma("gpsimd", f"w2g{slot}",
            w2g_s[slot][:].rearrange("p (c n) -> p c n", n=1024),
            w2_d[e_].rearrange("(c p) n -> p c n", p=128)[:, g * 8:(g + 1) * 8, :],
            writes=[f"w2g{slot}"])

    gseq = [(e_, g) for e_ in range(NE) for g in range(4)]
    NGI = len(gseq)

    def x2ts_load(s_):
        sl = s_ % 2
        dma("sync", f"x2tsl{sl}", x2ts_s[sl][:].rearrange("p (c n) -> p c n", n=SUP),
            x2t_d.rearrange("(c p) n -> p c n", p=128)[:, :, s_ * SUP:(s_ + 1) * SUP],
            writes=[f"x2ts{sl}"],
            waits=PH_BARA + [("x2tw", S.mark("x2tw"))])

    def router(s_):
        rpar = s_ % 2
        rx2t = x2ts_s[rpar]
        rcc = c_s[rpar]
        for tcl in range(NT):
            for di in range(DC):
                mm(psC[:, 0:NE],
                   rx2t[:, di * SUP + tcl * 128: di * SUP + tcl * 128 + 128],
                   rwb_s[:, di * NE:(di + 1) * NE],
                   start=(di == 0), stop=(di == DC - 1),
                   reads=[f"x2ts{rpar}", "rwb"], writes=["psC"],
                   inc=(di == DC - 1))
            S.emit("vector", lambda e, o=lg_s[:], a=psC[:, 0:NE], b=rbb_s[:]:
                   e.tensor_add(o, a, b),
                   reads=["psC", "rbb"], writes=["lg"],
                   waits=PH_BARA if s_ == 0 and tcl == 0 else ())
            m1, m2 = s1_s[:, 0:1], s1_s[:, 1:2]
            nd, e2v, den, g1, g2 = (s1_s[:, 2:3], s1_s[:, 3:4], s1_s[:, 4:5],
                                    s1_s[:, 5:6], s1_s[:, 6:7])
            S.emit("vector", lambda e, o=m1, i=lg_s[:]: e.reduce_max(o, i, axis=AX),
                   reads=["lg"], writes=["s1.0"])
            S.emit("vector", lambda e, o=oh1_s[:], i=lg_s[:], s=m1:
                   e.tensor_scalar(o, i, s, None, op0=ALU.is_equal),
                   reads=["lg", "s1.0"], writes=["oh1"])
            S.emit("vector", lambda e, o=msk_s[:], i=oh1_s[:], b=lg_s[:]:
                   e.scalar_tensor_tensor(o, i, -1e30, b, op0=ALU.mult, op1=ALU.add),
                   reads=["oh1", "lg"], writes=["msk"])
            S.emit("vector", lambda e, o=m2, i=msk_s[:]: e.reduce_max(o, i, axis=AX),
                   reads=["msk"], writes=["s1.1"])
            S.emit("vector", lambda e, o=oh2_s[:], i=msk_s[:], s=m2:
                   e.tensor_scalar(o, i, s, None, op0=ALU.is_equal),
                   reads=["msk", "s1.1"], writes=["oh2"])
            S.emit("vector", lambda e, o=nd, a=m2, b=m1: e.tensor_sub(o, a, b),
                   reads=["s1.0", "s1.1"], writes=["s1.2"])
            S.emit("scalar", lambda e, o=e2v, i=nd: e.activation(o, i, AF.Exp),
                   reads=["s1.2"], writes=["s1.3"])
            S.emit("vector", lambda e, o=den, i=e2v:
                   e.tensor_scalar(o, i, 1.0, None, op0=ALU.add),
                   reads=["s1.3"], writes=["s1.4"])
            S.emit("vector", lambda e, o=g1, i=den: e.reciprocal(o, i),
                   reads=["s1.4"], writes=["s1.5"])
            S.emit("vector", lambda e, o=g2, i=g1:
                   e.tensor_scalar(o, i, -1.0, 1.0, op0=ALU.mult, op1=ALU.add),
                   reads=["s1.5"], writes=["s1.6"])
            S.emit("vector", lambda e, o=oh1_s[:], i=oh1_s[:], s=g1:
                   e.tensor_scalar(o, i, s, None, op0=ALU.mult),
                   reads=["oh1", "s1.5"], writes=["oh1"])
            S.emit("vector",
                   lambda e, o=rcc[:, tcl * NE:(tcl + 1) * NE], i=oh2_s[:], s=g2,
                          b=oh1_s[:]:
                   e.scalar_tensor_tensor(o, i, s, b, op0=ALU.mult, op1=ALU.add),
                   reads=["oh2", "s1.6", "oh1"], writes=[f"c{rpar}.{tcl}"])

    def final_ln(s_, tcl, last_gi):
        S.emit("vector",
               lambda e, o=v2_s[:], a=x2ks_s[:, tcl * D:(tcl + 1) * D],
                      b=acc_s[tcl][:]: e.tensor_add(o, a, b),
               reads=["x2ks", f"acc{tcl}"], writes=["v2b"])
        oslot = tcl % 2
        extra = []
        if s_ == 0 and tcl < 2:
            extra = [("x2tw", S.mark("x2tw")), ("x2kw", S.mark("x2kw"))]
        ln_chain("v2b", v2_s, outst_s[oslot], [f"outst{oslot}"], extra_id_waits=extra)
        row = s_ * SUP + tcl * 128
        dma("sync", f"outw{oslot}", out_d[row:row + 128, :], outst_s[oslot][:],
            reads=[f"outst{oslot}"])

    x2ts_load(0)
    for s_ in range(NS):
        par = s_ % 2
        xk = f"x2ts{par}"
        x2t = x2ts_s[par]
        cc = c_s[par]
        if s_ + 1 < NS:
            x2ts_load(s_ + 1)
        dma("sync", "x2ks", x2ks_s[:].rearrange("p (c n) -> p c n", n=D),
            x2k_d.rearrange("(c p) n -> p c n", p=128)[:, s_ * NT:(s_ + 1) * NT, :],
            writes=["x2ks"],
            waits=PH_BARA + [("x2kw", S.mark("x2kw"))])
        if s_ == 0:
            router(0)

        for ei, (e_, g) in enumerate(gseq):
            gi = s_ * NGI + ei
            slot = gi % 2
            if ei == NGI // 2 and s_ + 1 < NS:
                router(s_ + 1)
            if gi == 0:
                wg_dma(e_, g, 0, waits=PH_BARA)
                nx = gseq[1]
                wg_dma(nx[0], nx[1], 1)
            if ei + 1 < NGI:
                nx = gseq[ei + 1]
            elif s_ + 1 < NS:
                nx = gseq[0]
            else:
                nx = None
            if gi >= 1 and nx is not None:
                wg_dma(nx[0], nx[1], (gi + 1) % 2)

            if g == 0:
                se = s_ * NE + e_
                dma("sync", f"b2r{e_ % 2}", b2r_s[e_ % 2][0:1, :], b2_d[e_:e_ + 1, :],
                    writes=[f"b2r{e_ % 2}"])
                for nb in range(2):
                    mm(psC[:], ones_s[0:1, :],
                       b2r_s[e_ % 2][0:1, nb * 512:(nb + 1) * 512],
                       start=True, stop=True,
                       reads=["ones", f"b2r{e_ % 2}"], writes=["psC"], inc=True)
                    S.emit("vector",
                           lambda e, o=b2b_s[:, nb * 512:(nb + 1) * 512], i=psC[:]:
                           e.tensor_copy(o, i), reads=["psC"], writes=["b2b"])

            hp = [(psA, "psA"), (psA2, "psA2")]
            for fc in range(8):
                ps, pk = hp[fc % 2]
                for di in range(DC):
                    lhsT = w1g_s[slot][:, di * 1024 + fc * 128: di * 1024 + fc * 128 + 128]
                    for nb in range(2):
                        mm(ps[:, nb * 512:(nb + 1) * 512], lhsT,
                           x2t[:, di * SUP + nb * 512: di * SUP + (nb + 1) * 512],
                           start=(di == 0), stop=(di == DC - 1),
                           reads=[f"w1g{slot}", xk],
                           writes=[f"{pk}.0", f"{pk}.1"] if (di == 0 and nb == 0) else
                                  [f"{pk}.{nb}"],
                           inc=(di == DC - 1 and nb == 1))
                bcol = b1_s[:, e_ * FC + g * 8 + fc: e_ * FC + g * 8 + fc + 1]
                S.emit("scalar", lambda e, o=h_s[fc][:], i=ps[:], b=bcol:
                       e.activation(o, i, AF.Relu, bias=b),
                       reads=[f"{pk}.0", f"{pk}.1", "b1"], writes=[f"h{fc}"])

            for tcl in range(8):
                for nb in range(2):
                    for fc in range(8):
                        mm(psB[:, nb * 512:(nb + 1) * 512],
                           h_s[fc][:, tcl * 128:(tcl + 1) * 128],
                           w2g_s[slot][:, fc * 1024 + nb * 512: fc * 1024 + (nb + 1) * 512],
                           start=(fc == 0), stop=(fc == 7),
                           reads=[f"h{fc}", f"w2g{slot}"], writes=[f"psB.{nb}"],
                           inc=(fc == 7))
                ccol = cc[:, tcl * NE + e_: tcl * NE + e_ + 1]
                if g == 0 and e_ == 0:
                    S.emit("vector", lambda e, o=acc_s[tcl][:], i=b2b_s[:], s=ccol:
                           e.tensor_scalar(o, i, s, None, op0=ALU.mult),
                           reads=["b2b", f"c{par}.{tcl}"], writes=[f"acc{tcl}"],
                           waits=PH_BARA if s_ == 0 else ())
                elif g == 0:
                    S.emit("vector", lambda e, o=acc_s[tcl][:], i=b2b_s[:], s=ccol:
                           e.scalar_tensor_tensor(o, i, s, o, op0=ALU.mult, op1=ALU.add),
                           reads=["b2b", f"c{par}.{tcl}", f"acc{tcl}"],
                           writes=[f"acc{tcl}"])
                for nb in range(2):
                    S.emit("vector",
                           lambda e, o=acc_s[tcl][:, nb * 512:(nb + 1) * 512],
                                  i=psB[:, nb * 512:(nb + 1) * 512], s=ccol:
                           e.scalar_tensor_tensor(o, i, s, o, op0=ALU.mult, op1=ALU.add),
                           reads=[f"psB.{nb}", f"c{par}.{tcl}", f"acc{tcl}"],
                           writes=[f"acc{tcl}"])
                if ei == NGI - 1:
                    final_ln(s_, tcl, gi)

    S.emit("sync", lambda e: e.nop(),
           waits=[(s, S.mark(s)) for s in
                  ("outw0", "outw1", "tapw", "x2kw", "x2tw") if S.mark(s)])

    names = set(S.cnt)
    for eng in Sched.ENGINES:
        for ws, _, _ in S.steps[eng]:
            names.update(s for s, _ in ws)
    with contextlib.ExitStack() as stack:
        sems = {s: stack.enter_context(nc.semaphore(f"sem_{s}")) for s in sorted(names)}
        block = stack.enter_context(nc.Block())
        for eng in Sched.ENGINES:
            def mk_section(steps):
                def _(e):
                    for ws, fn, incs in steps:
                        for s, v in ws:
                            e.wait_ge(sems[s], v)
                        inst = fn(e)
                        for s, v in incs:
                            inst.then_inc(sems[s], v)
                return _
            getattr(block, eng)(mk_section(S.steps[eng]))
    return nc


def kernel(L0_attn_X, L0_attn_W_Q, L0_attn_W_K, L0_attn_W_V,
           L0_ffn_router_w, L0_ffn_router_b, L0_ffn_W1, L0_ffn_b1,
           L0_ffn_W2, L0_ffn_b2, _taps=(), _tap_cb=None):
    cq, sq, ck, sk, g, ident, onesf, perm = _constants()
    f32 = np.float32
    wq_p = np.ascontiguousarray(np.asarray(L0_attn_W_Q, f32)[:, perm])
    wk_p = np.ascontiguousarray(np.asarray(L0_attn_W_K, f32)[:, perm])
    wv = np.ascontiguousarray(np.asarray(L0_attn_W_V, f32))
    rbb = np.ascontiguousarray(np.broadcast_to(
        np.asarray(L0_ffn_router_b, f32)[None, :], (128, NE)))
    w1 = np.ascontiguousarray(np.asarray(L0_ffn_W1, f32))
    w2 = np.ascontiguousarray(np.asarray(L0_ffn_W2, f32))
    b1 = np.ascontiguousarray(
        np.asarray(L0_ffn_b1, f32).reshape(NE, FC, 128).transpose(2, 0, 1)
        .reshape(128, NE * FC))
    b2 = np.ascontiguousarray(np.asarray(L0_ffn_b2, f32))
    rw = np.ascontiguousarray(np.asarray(L0_ffn_router_w, f32))
    X = np.asarray(L0_attn_X, f32)

    nc = build(taps=_taps)
    in_maps = []
    for b in range(8):
        xb = np.ascontiguousarray(X[b])
        in_maps.append({
            "xt": np.ascontiguousarray(xb.T), "xtok": xb,
            "wq": wq_p, "wk": wk_p, "wv": wv,
            "cq": cq, "sq": sq, "ck": ck, "sk": sk, "gband": g,
            "ident": ident, "onesf": onesf, "rw": rw, "rbb": rbb,
            "w1": w1, "b1": b1, "w2": w2, "b2": b2,
        })
    res = run_bass_kernel_spmd(nc, in_maps, core_ids=list(range(8)))
    if _tap_cb is not None:
        _tap_cb(res)
    return np.stack([res.results[b]["out"] for b in range(8)], axis=0)
